# version 03
# baseline (speedup 1.0000x reference)
import math
import numpy as np
import ml_dtypes
import concourse.bass as bass
import concourse.mybir as mybir
from concourse.bass_utils import run_bass_kernel_spmd

F32 = mybir.dt.float32
BF16 = mybir.dt.bfloat16
I32 = mybir.dt.int32
Alu = mybir.AluOpType
Act = mybir.ActivationFunctionType
AxX = mybir.AxisListType.X

NCORES = 8
TCORE = 4096
TT = 512
NT = TCORE // TT
D = 1024
DI = 2048
DS = 16
DFF = 4096
NCT = DI // 128
KD = D // 128
KFF = DFF // 128
UPC = TT + 4
INV2PI = float(1.0 / (2.0 * math.pi))
TWOPI = float(2.0 * math.pi)

DEBUG_TAPS = False

ENGINES = ("sync", "scalar", "vector", "tensor", "gpsimd")


class Prog:

    def __init__(self):
        self.ops = {e: [] for e in ENGINES}
        self.cnt = {}

    def emit(self, eng, fn, waits=(), inc=None, incby=1):
        ws = [(s, v) for (s, v) in waits if v > 0]
        self.ops[eng].append((fn, ws, inc, incby))
        if inc is not None:
            self.cnt[inc] = self.cnt.get(inc, 0) + incby

    def cur(self, sem):
        return self.cnt.get(sem, 0)

    def snap(self):
        return [(s, v) for s, v in self.cnt.items()]

    def realize(self, block, sems):
        for engname in ENGINES:
            ops = self.ops[engname]
            if not ops:
                continue

            def make(engops):
                def _f(eng):
                    waited = {}
                    for fn, ws, inc, incby in engops:
                        for (sn, v) in ws:
                            if waited.get(sn, -1) < v:
                                eng.wait_ge(sems[sn], v)
                                waited[sn] = v
                        instr = fn(eng)
                        if inc is not None:
                            instr.then_inc(sems[inc], incby)
                return _f

            getattr(block, engname)(make(ops))



def build_L1():
    nc = bass.Bass("TRN2", target_bir_lowering=False)
    dt_in = lambda n, s, d=F32: nc.dram_tensor(n, s, d, kind="ExternalInput").ap()
    dt_out = lambda n, s, d=F32: nc.dram_tensor(n, s, d, kind="ExternalOutput").ap()
    dt_int = lambda n, s, d=F32: nc.dram_tensor(n, s, d).ap()

    x_h = dt_in("x", [TCORE, D])
    xhalo_h = dt_in("xhalo", [128, D])
    Win_h = dt_in("W_in", [D, 2 * DI])
    Wdt_h = dt_in("W_dt", [DI, DI])
    Wx_h = dt_in("W_x16", [DI, DS])
    W1_h = dt_in("W1", [D, DFF])
    W2_h = dt_in("W2", [DFF, D])
    binh_h = dt_in("b_in_h", [128, NCT])
    binz_h = dt_in("b_in_z", [128, NCT])
    bdt_h = dt_in("b_dt", [128, NCT])
    bx_h = dt_in("b_x16", [128, 1])
    cw_h = dt_in("conv_w", [128, NCT * 4])
    cb_h = dt_in("conv_b", [128, NCT])
    Alog_h = dt_in("A_log", [128, NCT * DS])
    Dv_h = dt_in("Dv", [128, NCT])
    b1_h = dt_in("b1", [128, KFF])
    th_h = dt_in("theta", [128, KFF])
    wr_h = dt_in("wreal", [128, KFF])
    wi_h = dt_in("wimag", [128, KFF])
    b2_h = dt_in("b2", [128, KD])
    identbf_h = dt_in("identbf", [128, 128], BF16)
    identf_h = dt_in("identf", [128, 128])
    sels_h = dt_in("sels", [DS, DS * 128], BF16)
    hmask_h = dt_in("halo_mask", [128, 1])

    o_exch = dt_out("o_exch", [128, 2 * NCT * DS])
    o_delta = dt_out("o_delta", [NT, 128, NCT * TT], BF16)
    o_y1 = dt_out("o_y1", [NT, 128, NCT * TT], BF16)
    o_siluz = dt_out("o_siluz", [NT, 128, NCT * TT], BF16)
    o_acc = dt_out("o_acc", [NT, 128, 4 * D])
    xn_sp = dt_int("xn_sp", [NT, 128, KD * TT], BF16)
    if DEBUG_TAPS:
        o_B = dt_out("o_B", [NT, 16, TT])
        o_Bbc = dt_out("o_Bbc", [NT, 128, DS * TT], BF16)
        o_hc = dt_out("o_hc", [NT, 128, NCT * DS])
        o_u = dt_out("o_u", [NT, 128, NCT * TT], BF16)
        o_xn = dt_out("o_xn", [NT, 128, KD * TT], BF16)
        o_actq = dt_out("o_actq", [NT, 128, KFF * TT], BF16)
        o_ffn = dt_out("o_ffn", [NT, 128, KD * TT])

    from contextlib import ExitStack
    es = ExitStack()
    S = lambda n, sh, d=F32: es.enter_context(nc.sbuf_tensor("sb_" + n, sh, d))

    binh = S("binh", [128, NCT]); binz = S("binz", [128, NCT])
    bdt = S("bdt", [128, NCT]); bx = S("bx", [128, 1])
    cw = S("cw", [128, NCT * 4]); cb = S("cb", [128, NCT])
    Asb = S("Asb", [128, NCT * DS]); Dv = S("Dv", [128, NCT])
    b1 = S("b1", [128, KFF]); th = S("th", [128, KFF])
    wr = S("wr", [128, KFF]); wi = S("wi", [128, KFF])
    b2 = S("b2", [128, KD])
    identbf = S("identbf", [128, 128], BF16)
    identf = S("identf", [128, 128])
    sels = S("sels", [DS, DS * 128], BF16)
    hmask = S("hmask", [128, 1])
    c1 = S("c1", [128, KFF]); biasm = S("biasm", [128, KFF])
    hcar = S("hcar", [128, NCT * DS])
    Gend = S("Gend", [128, NCT])
    tA = S("tA", [128, KFF]); tB = S("tB", [128, KFF])
    tC = S("tC", [128, KFF]); tI = S("tI", [128, KFF], I32)
    aend = S("aend", [128, NCT * DS])
    lnc = S("lnc", [128, 16])
    Gtmp = S("Gtmp", [128, NCT])

    WA = S("WA", [128, 32768], BF16)
    AR1 = S("AR1", [128, 16448], BF16)
    AR2 = S("AR2", [128, 16384], BF16)
    AR3 = S("AR3", [128, 16384], BF16)
    AR4 = S("AR4", [128, 14848], BF16)
    xn_fm = S("xn_fm", [128, KD * TT], BF16)
    WXs = S("WXs", [128, NCT * DS], BF16)

    u_pre = AR1[:, 0:NCT * UPC]
    u_bf = AR1[:, 8256:8256 + NCT * TT]
    d_bf = AR2[:, 0:NCT * TT]
    y1_bf = AR2[:, NCT * TT:2 * NCT * TT]
    sz_bf = AR3[:, 0:NCT * TT]
    Bbc = AR3[:, NCT * TT:2 * NCT * TT]
    xn_tok = AR4[:, 0:4 * D]
    xtok_bf = AR4[:, 4 * D:8 * D]
    abuf = AR4[:, 8192:8704]
    bbuf = AR4[:, 8704:9216]
    hbuf = AR4[:, 9216:9728]
    du = AR4[:, 9728:10240]
    cbuf = AR4[:, 10240:11264].bitcast(F32)
    spsc = AR4[:, 10240:11264].bitcast(F32)
    Bsb = AR4[:, 11264:12288].bitcast(F32)
    Bsb_bf = AR4[:, 12288:12800]
    sqs = AR4[:, 12800:14848].bitcast(F32)
    actq = AR1[:, 0:KFF * TT]
    ffn_fm = AR2[:, 0:2 * KD * TT].bitcast(F32)
    xacc = AR2[:, 2 * KD * TT:4 * KD * TT].bitcast(F32)
    W2s = AR3[:, 0:2 * D]
    qm = AR3[:, 2048:3072].bitcast(F32)
    qh = AR3[:, 3072:4096].bitcast(F32)
    qf = AR3[:, 4096:5120].bitcast(F32)
    qw = AR3[:, 5120:6144].bitcast(F32)
    qs = AR3[:, 6144:7168].bitcast(F32)
    qi = AR3[:, 7168:8192].bitcast(I32)

    sems = {}
    semstack = ExitStack()
    for name in ("dS", "dG", "sT", "sS", "sV"):
        sems[name] = semstack.enter_context(nc.semaphore(name))

    p = Prog()

    def ld(dst, src):
        p.emit("sync", lambda e, d=dst, s=src: e.dma_start(out=d, in_=s),
               inc="dS", incby=16)

    ld(binh[:], binh_h[:]); ld(binz[:], binz_h[:]); ld(bdt[:], bdt_h[:])
    ld(bx[:], bx_h[:]); ld(cw[:], cw_h[:]); ld(cb[:], cb_h[:])
    ld(Asb[:], Alog_h[:]); ld(Dv[:], Dv_h[:]); ld(b1[:], b1_h[:])
    ld(th[:], th_h[:]); ld(wr[:], wr_h[:]); ld(wi[:], wi_h[:])
    ld(b2[:], b2_h[:]); ld(identbf[:], identbf_h[:]); ld(identf[:], identf_h[:])
    ld(sels[:], sels_h[:]); ld(hmask[:], hmask_h[:])
    W = p.snap()

    def vf(fn):
        p.emit("vector", fn, waits=[("sV", p.cur("sV")), ("sS", p.cur("sS"))], inc="sV")

    p.emit("scalar", lambda e: e.activation(Asb[:], Asb[:], Act.Exp), waits=W, inc="sS")
    vf(lambda e: e.tensor_scalar(Asb[:], Asb[:], -1.0, None, Alu.mult))
    vf(lambda e: e.tensor_scalar(tA[:], th[:], INV2PI, None, Alu.mult))
    vf(lambda e: e.tensor_copy(tI[:], tA[:]))
    vf(lambda e: e.tensor_copy(tC[:], tI[:]))
    vf(lambda e: e.tensor_tensor(tA[:], tA[:], tC[:], Alu.subtract))
    sv0 = p.cur("sV")
    p.emit("scalar", lambda e: e.activation(tA[:], tA[:], Act.Sin, scale=TWOPI),
           waits=[("sV", sv0)], inc="sS")
    vf(lambda e: e.tensor_tensor(tB[:], tA[:], wi[:], Alu.mult))
    vf(lambda e: e.tensor_scalar(tA[:], th[:], INV2PI, 0.25, Alu.mult, Alu.add))
    vf(lambda e: e.tensor_copy(tI[:], tA[:]))
    vf(lambda e: e.tensor_copy(tC[:], tI[:]))
    vf(lambda e: e.tensor_tensor(tA[:], tA[:], tC[:], Alu.subtract))
    sv1 = p.cur("sV")
    p.emit("scalar", lambda e: e.activation(tA[:], tA[:], Act.Sin, scale=TWOPI),
           waits=[("sV", sv1)], inc="sS")
    vf(lambda e: e.tensor_tensor(tA[:], tA[:], wr[:], Alu.mult))
    vf(lambda e: e.tensor_tensor(tA[:], tA[:], tB[:], Alu.add))
    vf(lambda e: e.tensor_tensor(tA[:], tA[:], wi[:], Alu.mult))
    vf(lambda e: e.tensor_scalar(c1[:], tA[:], 0.1, None, Alu.mult))
    vf(lambda e: e.tensor_scalar(tB[:], b1[:], 0.1 * INV2PI, None, Alu.mult))
    vf(lambda e: e.tensor_scalar(biasm[:], th[:], INV2PI, None, Alu.mult))
    vf(lambda e: e.tensor_tensor(biasm[:], biasm[:], tB[:], Alu.add))
    vf(lambda e: e.memset(hcar[:], 0.0))
    vf(lambda e: e.memset(Gend[:], 0.0))
    vf(lambda e: e.memset(AR1[:, 0:NCT * UPC], 0.0))

    PS = [es.enter_context(nc.psum_tensor(f"ps{i}", [128, TT], F32)) for i in range(8)]

    def mixer_tile(j):
        tt = 128 if j < 0 else TT
        ntb = tt // 128

        Wm = p.snap()
        for k in range(KD):
            p.emit("gpsimd", lambda e, k=k: e.dma_start(
                out=WA[:, k * (2 * DI):(k + 1) * (2 * DI)],
                in_=Win_h[k * 128:(k + 1) * 128, :]), waits=Wm, inc="dG", incby=16)
        if j < 0:
            p.emit("gpsimd", lambda e: e.dma_start(
                out=xtok_bf[:, 0:D], in_=xhalo_h[:]), waits=Wm, inc="dG", incby=16)
        else:
            for b in range(4):
                p.emit("gpsimd", lambda e, j=j, b=b: e.dma_start(
                    out=xtok_bf[:, b * D:(b + 1) * D],
                    in_=x_h[j * TT + b * 128: j * TT + (b + 1) * 128, :]),
                    waits=Wm, inc="dG", incby=16)
        Wx_ = p.snap()

        for b in range(ntb):
            xb = xtok_bf[:, b * D:(b + 1) * D]
            p.emit("scalar", lambda e, xb=xb, b=b: e.activation(
                sqs[:], xb, Act.Square, accum_out=lnc[:, b + 4:b + 5]),
                waits=Wx_, inc="sS")
            p.emit("vector", lambda e, xb=xb, b=b: e.reduce_sum(
                lnc[:, b:b + 1], xb, axis=AxX), waits=Wx_, inc="sV")
        def lf(eng, fn):
            p.emit(eng, fn, waits=[("sV", p.cur("sV")), ("sS", p.cur("sS"))],
                   inc="sV" if eng == "vector" else "sS")
        nb = ntb
        lf("vector", lambda e: e.tensor_scalar(
            lnc[:, 0:nb], lnc[:, 0:nb], 1.0 / D, None, Alu.mult))
        lf("vector", lambda e: e.tensor_scalar(
            lnc[:, 4:4 + nb], lnc[:, 4:4 + nb], 1.0 / D, None, Alu.mult))
        lf("vector", lambda e: e.tensor_tensor(
            lnc[:, 8:8 + nb], lnc[:, 0:nb], lnc[:, 0:nb], Alu.mult))
        lf("vector", lambda e: e.tensor_tensor(
            lnc[:, 4:4 + nb], lnc[:, 4:4 + nb], lnc[:, 8:8 + nb], Alu.subtract))
        lf("vector", lambda e: e.tensor_scalar(
            lnc[:, 4:4 + nb], lnc[:, 4:4 + nb], 1.0, 1e-5, Alu.mult, Alu.add))
        lf("scalar", lambda e: e.activation(lnc[:, 4:4 + nb], lnc[:, 4:4 + nb], Act.Ln))
        lf("scalar", lambda e: e.activation(
            lnc[:, 4:4 + nb], lnc[:, 4:4 + nb], Act.Exp, scale=-0.5))
        lf("vector", lambda e: e.tensor_tensor(
            lnc[:, 8:8 + nb], lnc[:, 0:nb], lnc[:, 4:4 + nb], Alu.mult))
        lf("vector", lambda e: e.tensor_scalar(
            lnc[:, 8:8 + nb], lnc[:, 8:8 + nb], -1.0, None, Alu.mult))
        sv = p.cur("sV")
        for b in range(ntb):
            p.emit("scalar", lambda e, b=b: e.activation(
                xn_tok[:, b * D:(b + 1) * D], xtok_bf[:, b * D:(b + 1) * D], Act.Identity,
                bias=lnc[:, b + 8:b + 9], scale=lnc[:, b + 4:b + 5]),
                waits=[("sV", sv)], inc="sS")
        Wn = p.snap()

        ps7bf = PS[7][:].bitcast(BF16)
        for fb in range(KD):
            for tb in range(ntb):
                tw = (Wn if fb == 0 else [("sV", p.cur("sV"))]) if tb == 0 else ()
                p.emit("tensor", lambda e, fb=fb, tb=tb, ps7bf=ps7bf: e.transpose(
                    ps7bf[:, tb * 128:(tb + 1) * 128],
                    xn_tok[:, tb * D + fb * 128: tb * D + (fb + 1) * 128],
                    identbf[:]), waits=tw, inc="sT")
            st = p.cur("sT")
            p.emit("vector", lambda e, fb=fb, ntb=ntb, ps7bf=ps7bf: e.tensor_copy(
                xn_fm[:, fb * TT: fb * TT + ntb * 128], ps7bf[:, 0:ntb * 128]),
                waits=[("sT", st)], inc="sV")
        Wt = p.snap()
        if j >= 0:
            p.emit("gpsimd", lambda e, j=j: e.dma_start(
                out=xn_sp[j], in_=xn_fm[:]), waits=Wt, inc="dG", incby=16)

        evs = p.cur("sS")
        for m in range(32):
            bank = PS[m % 2]
            mywait = list(Wt)
            if m >= 2:
                mywait.append(("sS", evs + (m - 1)))
            for k in range(KD):
                p.emit("tensor", lambda e, m=m, k=k, bank=bank, ntb=ntb: e.matmul(
                    bank[:, 0:ntb * 128],
                    WA[:, k * (2 * DI) + m * 128: k * (2 * DI) + (m + 1) * 128],
                    xn_fm[:, k * TT: k * TT + ntb * 128],
                    start=(k == 0), stop=(k == KD - 1)),
                    waits=mywait if k == 0 else (), inc="sT")
            st = p.cur("sT")
            if m < NCT:
                p.emit("scalar", lambda e, m=m, bank=bank, ntb=ntb: e.activation(
                    u_pre[:, m * UPC + 3: m * UPC + 3 + ntb * 128],
                    bank[:, 0:ntb * 128], Act.Identity, bias=binh[:, m:m + 1]),
                    waits=[("sT", st)], inc="sS")
            else:
                mz = m - NCT
                p.emit("scalar", lambda e, mz=mz, bank=bank, ntb=ntb: e.activation(
                    sz_bf[:, mz * TT: mz * TT + ntb * 128],
                    bank[:, 0:ntb * 128], Act.Silu, bias=binz[:, mz:mz + 1]),
                    waits=[("sT", st)], inc="sS")
        Wz = p.snap()

        if j < 0:
            for ct in range(NCT):
                p.emit("vector", lambda e, ct=ct: e.tensor_scalar(
                    u_pre[:, ct * UPC: ct * UPC + 3],
                    u_pre[:, ct * UPC + 128: ct * UPC + 131],
                    hmask[:, 0:1], None, Alu.mult), waits=Wz, inc="sV")
            return

        for k in range(NCT):
            p.emit("gpsimd", lambda e, k=k: e.dma_start(
                out=WA[:, k * DI:(k + 1) * DI],
                in_=Wdt_h[k * 128:(k + 1) * 128, :]), waits=Wz, inc="dG", incby=16)
        for k in range(NCT):
            p.emit("gpsimd", lambda e, k=k: e.dma_start(
                out=WXs[:, k * DS:(k + 1) * DS],
                in_=Wx_h[k * 128:(k + 1) * 128, :]), inc="dG", incby=16)
        Wwd = p.snap()

        for ct in range(NCT):
            base = ct * UPC
            p.emit("vector", lambda e, ct=ct, base=base: e.tensor_scalar(
                cbuf[:], u_pre[:, base:base + TT], cw[:, ct * 4:ct * 4 + 1], None, Alu.mult),
                waits=Wz if ct == 0 else (), inc="sV")
            for k in range(1, 4):
                p.emit("vector", lambda e, ct=ct, base=base, k=k: e.scalar_tensor_tensor(
                    cbuf[:], u_pre[:, base + k:base + k + TT],
                    cw[:, ct * 4 + k:ct * 4 + k + 1], cbuf[:], Alu.mult, Alu.add), inc="sV")
            sv = p.cur("sV")
            p.emit("scalar", lambda e, ct=ct: e.activation(
                u_bf[:, ct * TT:(ct + 1) * TT], cbuf[:], Act.Silu, bias=cb[:, ct:ct + 1]),
                waits=[("sV", sv)], inc="sS")
            ss = p.cur("sS")
            p.emit("vector", lambda e, base=base: e.tensor_scalar(
                u_pre[:, base:base + 3], u_pre[:, base + TT:base + TT + 3],
                1.0, None, Alu.mult), waits=[("sS", ss)], inc="sV")
        Wu = p.snap()

        evs = p.cur("sS")
        for m in range(NCT):
            bank = PS[2 + (m % 2)]
            mywait = Wu + Wwd
            if m >= 2:
                mywait = mywait + [("sS", evs + 2 * (m - 1))]
            for k in range(NCT):
                p.emit("tensor", lambda e, m=m, k=k, bank=bank: e.matmul(
                    bank[:], WA[:, k * DI + m * 128: k * DI + (m + 1) * 128],
                    u_bf[:, k * TT:(k + 1) * TT],
                    start=(k == 0), stop=(k == NCT - 1)),
                    waits=mywait if k == 0 else (), inc="sT")
            st = p.cur("sT")
            p.emit("scalar", lambda e, m=m, bank=bank: e.activation(
                spsc[:], bank[:], Act.Exp, bias=bdt[:, m:m + 1]),
                waits=[("sT", st)], inc="sS")
            p.emit("scalar", lambda e, m=m: e.activation(
                d_bf[:, m * TT:(m + 1) * TT], spsc[:], Act.Ln, bias=1.0), inc="sS")
        Wd = p.snap()

        for ct in range(NCT):
            p.emit("vector", lambda e, ct=ct: e.reduce_sum(
                Gtmp[:, ct:ct + 1], d_bf[:, ct * TT:(ct + 1) * TT], axis=AxX),
                waits=Wd if ct == 0 else (), inc="sV")
        p.emit("vector", lambda e: e.tensor_tensor(
            Gend[:], Gend[:], Gtmp[:], Alu.add),
            waits=[("sV", p.cur("sV"))], inc="sV")

        for k in range(NCT):
            p.emit("tensor", lambda e, k=k: e.matmul(
                PS[4][0:DS, :], WXs[:, k * DS:(k + 1) * DS],
                u_bf[:, k * TT:(k + 1) * TT],
                start=(k == 0), stop=(k == NCT - 1)),
                waits=Wu + Wwd if k == 0 else (), inc="sT")
        st = p.cur("sT")
        p.emit("scalar", lambda e: e.activation(
            Bsb[0:DS, :], PS[4][0:DS, :], Act.Identity, bias=bx[0:DS, 0:1]),
            waits=[("sT", st)], inc="sS")
        ss = p.cur("sS")
        p.emit("vector", lambda e: e.tensor_copy(Bsb_bf[0:DS, :], Bsb[0:DS, :]),
               waits=[("sS", ss)], inc="sV")
        svB = p.cur("sV")
        for s in range(DS):
            bw = [("sV", svB)] if s == 0 else [("sS", p.cur("sS"))]
            p.emit("tensor", lambda e, s=s: e.matmul(
                PS[5][:], sels[:, s * 128:(s + 1) * 128], Bsb_bf[0:DS, :],
                start=True, stop=True), waits=bw, inc="sT")
            st = p.cur("sT")
            p.emit("scalar", lambda e, s=s: e.activation(
                Bbc[:, s * TT:(s + 1) * TT], PS[5][:], Act.Identity),
                waits=[("sT", st)], inc="sS")
        Wb = p.snap()

        for ct in range(NCT):
            p.emit("vector", lambda e, ct=ct: e.tensor_tensor(
                du[:], d_bf[:, ct * TT:(ct + 1) * TT], u_bf[:, ct * TT:(ct + 1) * TT],
                Alu.mult), waits=Wb if ct == 0 else (), inc="sV")
            svdu = p.cur("sV")
            ybank = PS[6]
            for s in range(DS):
                idx = ct * DS + s
                p.emit("scalar", lambda e, ct=ct, idx=idx: e.activation(
                    abuf[:], d_bf[:, ct * TT:(ct + 1) * TT],
                    Act.Exp, scale=Asb[:, idx:idx + 1]),
                    waits=[("sV", p.cur("sV"))], inc="sS")
                p.emit("vector", lambda e, s=s: e.tensor_tensor(
                    bbuf[:], du[:], Bbc[:, s * TT:(s + 1) * TT],
                    Alu.mult), waits=[("sV", svdu)], inc="sV")
                ssa = p.cur("sS"); svb = p.cur("sV")
                p.emit("vector", lambda e, idx=idx: e.tensor_tensor_scan(
                    hbuf[:], abuf[:], bbuf[:], hcar[:, idx:idx + 1],
                    Alu.mult, Alu.add), waits=[("sS", ssa), ("sV", svb)], inc="sV")
                p.emit("vector", lambda e, idx=idx: e.tensor_scalar(
                    hcar[:, idx:idx + 1], hbuf[:, TT - 1:TT], 1.0, None,
                    Alu.mult), waits=[("sV", p.cur("sV"))], inc="sV")
                svh = p.cur("sV")
                p.emit("tensor", lambda e, s=s, ybank=ybank: e.matmul(
                    ybank[:], identbf[:], hbuf[:],
                    start=(s == 0), stop=(s == DS - 1)),
                    waits=[("sV", svh)], inc="sT")
            st = p.cur("sT")
            p.emit("vector", lambda e, ct=ct, ybank=ybank: e.scalar_tensor_tensor(
                y1_bf[:, ct * TT:(ct + 1) * TT], u_bf[:, ct * TT:(ct + 1) * TT],
                Dv[:, ct:ct + 1], ybank[:], Alu.mult, Alu.add),
                waits=[("sT", st)], inc="sV")
        Ws = p.snap()

        if DEBUG_TAPS:
            p.emit("gpsimd", lambda e, j=j: e.dma_start(out=o_B[j], in_=Bsb[0:DS, :]),
                   waits=Ws, inc="dG", incby=16)
            p.emit("gpsimd", lambda e, j=j: e.dma_start(out=o_Bbc[j], in_=Bbc),
                   waits=Ws, inc="dG", incby=16)
            p.emit("gpsimd", lambda e, j=j: e.dma_start(out=o_hc[j], in_=hcar[:]),
                   waits=Ws, inc="dG", incby=16)
            p.emit("gpsimd", lambda e, j=j: e.dma_start(out=o_u[j], in_=u_bf),
                   waits=Ws, inc="dG", incby=16)
            p.emit("gpsimd", lambda e, j=j: e.dma_start(out=o_xn[j], in_=xn_fm[:]),
                   waits=Ws, inc="dG", incby=16)
        p.emit("gpsimd", lambda e, j=j: e.dma_start(out=o_delta[j], in_=d_bf),
               waits=Ws, inc="dG", incby=16)
        p.emit("gpsimd", lambda e, j=j: e.dma_start(out=o_y1[j], in_=y1_bf),
               inc="dG", incby=16)
        p.emit("gpsimd", lambda e, j=j: e.dma_start(out=o_siluz[j], in_=sz_bf),
               inc="dG", incby=16)

    mixer_tile(-1)
    for j in range(NT):
        mixer_tile(j)

    Wg = p.snap()
    for ct in range(NCT):
        p.emit("scalar", lambda e, ct=ct: e.activation(
            aend[:, ct * DS:(ct + 1) * DS], Asb[:, ct * DS:(ct + 1) * DS],
            Act.Exp, scale=Gend[:, ct:ct + 1]), waits=Wg if ct == 0 else (), inc="sS")
    ss = p.cur("sS")
    p.emit("sync", lambda e: e.dma_start(out=o_exch[:, 0:NCT * DS], in_=aend[:]),
           waits=[("sS", ss)], inc="dS", incby=16)
    p.emit("sync", lambda e: e.dma_start(out=o_exch[:, NCT * DS:], in_=hcar[:]),
           waits=Wg, inc="dS", incby=16)

    def ffn_tile(j):
        Wm = p.snap()
        for k in range(KD):
            p.emit("gpsimd", lambda e, k=k: e.dma_start(
                out=WA[:, k * (2 * DI): k * (2 * DI) + DFF],
                in_=W1_h[k * 128:(k + 1) * 128, :]), waits=Wm, inc="dG", incby=16)
        p.emit("sync", lambda e, j=j: e.dma_start(out=xn_fm[:], in_=xn_sp[j]),
               waits=Wm, inc="dS", incby=16)
        for b in range(4):
            p.emit("sync", lambda e, j=j, b=b: e.dma_start(
                out=xacc[:, b * D:(b + 1) * D],
                in_=x_h[j * TT + b * 128: j * TT + (b + 1) * 128, :]),
                waits=Wm, inc="dS", incby=16)
        Wx_ = p.snap()

        bankfree = {}
        svq = {}
        for m in range(KFF):
            bank = PS[m % 2]
            mywait = list(Wx_)
            if m >= 2:
                mywait.append(("sS", bankfree[m - 2]))
                mywait.append(("sV", svq[m - 1]))
            for k in range(KD):
                p.emit("tensor", lambda e, m=m, k=k, bank=bank: e.matmul(
                    bank[:], WA[:, k * (2 * DI) + m * 128: k * (2 * DI) + (m + 1) * 128],
                    xn_fm[:, k * TT:(k + 1) * TT],
                    start=(k == 0), stop=(k == KD - 1)),
                    waits=mywait if k == 0 else (), inc="sT")
            st = p.cur("sT")
            qmw = [("sT", st), ("sV", svq.get(m - 1, 0))]
            if m == 0:
                qmw = qmw + list(Wx_)
            p.emit("scalar", lambda e, m=m, bank=bank: e.activation(
                qm[:], bank[:], Act.Identity, bias=biasm[:, m:m + 1], scale=0.1 * INV2PI),
                waits=qmw, inc="sS")
            p.emit("scalar", lambda e, m=m, bank=bank: e.activation(
                qh[:], bank[:], Act.Identity, bias=b1[:, m:m + 1]), inc="sS")
            bankfree[m] = p.cur("sS")
            ss = p.cur("sS")
            p.emit("vector", lambda e: e.tensor_copy(qi[:], qm[:]),
                   waits=[("sS", ss)], inc="sV")
            p.emit("vector", lambda e: e.tensor_copy(qf[:], qi[:]), inc="sV")
            p.emit("vector", lambda e: e.tensor_tensor(qw[:], qm[:], qf[:], Alu.subtract), inc="sV")
            sv = p.cur("sV")
            p.emit("scalar", lambda e: e.activation(qs[:], qw[:], Act.Sin, scale=TWOPI),
                   waits=[("sV", sv)], inc="sS")
            ss = p.cur("sS")
            p.emit("vector", lambda e, m=m: e.scalar_tensor_tensor(
                qw[:], qs[:], c1[:, m:m + 1], qh[:], Alu.mult, Alu.add),
                waits=[("sS", ss)], inc="sV")
            p.emit("vector", lambda e, m=m: e.tensor_scalar(
                actq[:, m * TT:(m + 1) * TT], qw[:], 0.0, None, Alu.max),
                waits=[("sV", p.cur("sV"))], inc="sV")
            svq[m] = p.cur("sV")
        Wq = p.snap()

        stbase = p.cur("sT")
        for k in range(KFF):
            p.emit("gpsimd", lambda e, k=k: e.dma_start(
                out=W2s[:, (k % 2) * D:(k % 2) * D + D],
                in_=W2_h[k * 128:(k + 1) * 128, :]),
                waits=list(Wq) if k < 2 else [("sT", stbase + 8 * (k - 1))],
                inc="dG", incby=16)
            dg = p.cur("dG")
            for mo in range(KD):
                p.emit("tensor", lambda e, k=k, mo=mo: e.matmul(
                    PS[mo][:], W2s[:, (k % 2) * D + mo * 128:(k % 2) * D + (mo + 1) * 128],
                    actq[:, k * TT:(k + 1) * TT],
                    start=(k == 0), stop=(k == KFF - 1)),
                    waits=([("dG", dg)] + list(Wq)) if mo == 0 else (), inc="sT")
        st = p.cur("sT")
        for mo in range(KD):
            p.emit("scalar", lambda e, mo=mo: e.activation(
                ffn_fm[:, mo * TT:(mo + 1) * TT], PS[mo][:], Act.Identity,
                bias=b2[:, mo:mo + 1]), waits=[("sT", st)] if mo == 0 else (), inc="sS")
        Wf = p.snap()

        for tb in range(4):
            for half in range(2):
                for fb in range(4):
                    mo = half * 4 + fb
                    tw = ((list(Wf) + [("sV", p.cur("sV"))])
                          if fb == 0 else ())
                    p.emit("tensor", lambda e, tb=tb, mo=mo, fb=fb: e.transpose(
                        PS[7][:, fb * 128:(fb + 1) * 128],
                        ffn_fm[:, mo * TT + tb * 128: mo * TT + (tb + 1) * 128],
                        identf[:]), waits=tw, inc="sT")
                st = p.cur("sT")
                p.emit("vector", lambda e, tb=tb, half=half: e.tensor_tensor(
                    xacc[:, tb * D + half * TT: tb * D + (half + 1) * TT],
                    xacc[:, tb * D + half * TT: tb * D + (half + 1) * TT],
                    PS[7][:], Alu.add), waits=[("sT", st)], inc="sV")
        sv = p.cur("sV")
        if DEBUG_TAPS:
            p.emit("gpsimd", lambda e, j=j: e.dma_start(out=o_actq[j], in_=actq),
                   waits=Wq, inc="dG", incby=16)
            p.emit("gpsimd", lambda e, j=j: e.dma_start(out=o_ffn[j], in_=ffn_fm),
                   waits=Wf, inc="dG", incby=16)
        p.emit("gpsimd", lambda e, j=j: e.dma_start(out=o_acc[j], in_=xacc),
               waits=[("sV", sv)], inc="dG", incby=16)

    for j in range(NT):
        ffn_tile(j)

    fin = [("dS", p.cur("dS")), ("dG", p.cur("dG"))]
    p.emit("gpsimd", lambda e: e.memset(lnc[:, 15:16], 0.0), waits=fin)

    with es, semstack, nc.Block() as block:
        p.realize(block, sems)
    return nc



def build_L2():
    nc = bass.Bass("TRN2", target_bir_lowering=False)
    dt_in = lambda n, s, d=F32: nc.dram_tensor(n, s, d, kind="ExternalInput").ap()
    dt_out = lambda n, s, d=F32: nc.dram_tensor(n, s, d, kind="ExternalOutput").ap()

    exg_h = dt_in("exg", [4, 128, 2 * NCT * DS])
    pmask_h = dt_in("pmask", [128, 4])
    d_h = dt_in("sp_delta", [NT, 128, NCT * TT], BF16)
    y1_h = dt_in("sp_y1", [NT, 128, NCT * TT], BF16)
    sz_h = dt_in("sp_siluz", [NT, 128, NCT * TT], BF16)
    acc_h = dt_in("sp_acc", [NT, 128, 4 * D])
    Wout_h = dt_in("W_out", [DI, D])
    bout_h = dt_in("b_out", [128, KD])
    Alog_h = dt_in("A_log", [128, NCT * DS])
    identf_h = dt_in("identf", [128, 128])
    o_out = dt_out("o_out", [TCORE, D])

    from contextlib import ExitStack
    es = ExitStack()
    S = lambda n, sh, d=F32: es.enter_context(nc.sbuf_tensor("sb_" + n, sh, d))

    Asb = S("Asb", [128, NCT * DS])
    bout = S("bout", [128, KD])
    identf = S("identf", [128, 128])
    pmask = S("pmask", [128, 4])
    exg = S("exg", [128, 4 * 2 * NCT * DS])
    Hst = S("Hst", [128, NCT * DS])
    onesm = S("onesm", [128, 4])
    Wo = S("Wo", [128, NCT * D], BF16)
    d_bf = S("d_bf", [128, NCT * TT], BF16)
    y1_bf = S("y1_bf", [128, NCT * TT], BF16)
    sz_bf = S("sz_bf", [128, NCT * TT], BF16)
    acc = S("acc", [128, 4 * D])
    y2 = S("y2", [128, NCT * TT], BF16)
    Gbuf = S("Gbuf", [128, TT])
    Grun = S("Grun", [128, NCT])
    Ebuf = S("Ebuf", [128, TT])
    y1f = S("y1f", [128, TT])
    attn = S("attn", [128, KD * TT])
    tmp = S("tmp", [128, NCT * DS])

    sems = {}
    semstack = ExitStack()
    for name in ("dS", "dG", "sT", "sS", "sV"):
        sems[name] = semstack.enter_context(nc.semaphore(name))
    p = Prog()

    def ld(dst, src, eng="sync", waits=()):
        sem = "dS" if eng == "sync" else "dG"
        p.emit(eng, lambda e, d=dst, s=src: e.dma_start(out=d, in_=s),
               waits=waits, inc=sem, incby=16)

    ld(Asb[:], Alog_h[:]); ld(bout[:], bout_h[:]); ld(identf[:], identf_h[:])
    ld(pmask[:], pmask_h[:])
    for i in range(4):
        ld(exg[:, i * 512:(i + 1) * 512], exg_h[i])
    for k in range(NCT):
        ld(Wo[:, k * D:(k + 1) * D], Wout_h[k * 128:(k + 1) * 128, :], "gpsimd")
    W = p.snap()

    p.emit("scalar", lambda e: e.activation(Asb[:], Asb[:], Act.Exp), waits=W, inc="sS")
    p.emit("vector", lambda e: e.tensor_scalar(Asb[:], Asb[:], -1.0, None, Alu.mult),
           waits=[("sS", p.cur("sS"))], inc="sV")
    p.emit("vector", lambda e: e.memset(Hst[:], 0.0), waits=W, inc="sV")
    p.emit("vector", lambda e: e.memset(Grun[:], 0.0), inc="sV")
    p.emit("vector", lambda e: e.tensor_scalar(onesm[:], pmask[:], -1.0, 1.0,
                                               Alu.mult, Alu.add), waits=W, inc="sV")
    NS = NCT * DS

    def hf(fn):
        p.emit("vector", fn, waits=[("sV", p.cur("sV"))], inc="sV")

    for i in range(3):
        hf(lambda e, i=i: e.scalar_tensor_tensor(
            tmp[:], exg[:, i * 512: i * 512 + NS], pmask[:, i:i + 1],
            onesm[:, i:i + 1].broadcast_to([128, NS]), Alu.mult, Alu.add))
        hf(lambda e: e.tensor_tensor(Hst[:], Hst[:], tmp[:], Alu.mult))
        hf(lambda e, i=i: e.tensor_scalar(
            tmp[:], exg[:, i * 512 + NS: (i + 1) * 512], pmask[:, i:i + 1], None,
            Alu.mult))
        hf(lambda e: e.tensor_tensor(Hst[:], Hst[:], tmp[:], Alu.add))
    Wp = p.snap()

    PS = [es.enter_context(nc.psum_tensor(f"ps{i}", [128, TT], F32)) for i in range(8)]

    for j in range(NT):
        Wm = p.snap() + Wp
        ld(d_bf[:], d_h[j], waits=Wm)
        ld(y1_bf[:], y1_h[j], waits=Wm)
        ld(sz_bf[:], sz_h[j], waits=Wm)
        ld(acc[:], acc_h[j], waits=Wm)
        Wx_ = p.snap() + Wp

        for ct in range(NCT):
            p.emit("vector", lambda e, ct=ct: e.tensor_tensor_scan(
                Gbuf[:], d_bf[:, ct * TT:(ct + 1) * TT], d_bf[:, ct * TT:(ct + 1) * TT],
                Grun[:, ct:ct + 1], Alu.bypass, Alu.add),
                waits=Wx_ if ct == 0 else [("sS", p.cur("sS"))], inc="sV")
            p.emit("vector", lambda e, ct=ct: e.tensor_scalar(
                Grun[:, ct:ct + 1], Gbuf[:, TT - 1:TT], 1.0, None,
                Alu.mult), waits=[("sV", p.cur("sV"))], inc="sV")
            p.emit("vector", lambda e, ct=ct: e.tensor_copy(
                y1f[:], y1_bf[:, ct * TT:(ct + 1) * TT]), inc="sV")
            for s in range(DS):
                idx = ct * DS + s
                sv = p.cur("sV")
                p.emit("scalar", lambda e, idx=idx: e.activation(
                    Ebuf[:], Gbuf[:], Act.Exp, scale=Asb[:, idx:idx + 1]),
                    waits=[("sV", sv)], inc="sS")
                ss = p.cur("sS")
                p.emit("vector", lambda e, idx=idx: e.scalar_tensor_tensor(
                    y1f[:], Ebuf[:], Hst[:, idx:idx + 1], y1f[:], Alu.mult, Alu.add),
                    waits=[("sS", ss)], inc="sV")
            p.emit("vector", lambda e, ct=ct: e.tensor_tensor(
                y2[:, ct * TT:(ct + 1) * TT], y1f[:], sz_bf[:, ct * TT:(ct + 1) * TT],
                Alu.mult), inc="sV")
        Wy = p.snap()

        evs = p.cur("sS")
        for mo in range(KD):
            bank = PS[mo % 2]
            mywait = list(Wy)
            if mo >= 2:
                mywait.append(("sS", evs + (mo - 1)))
            for k in range(NCT):
                p.emit("tensor", lambda e, mo=mo, k=k, bank=bank: e.matmul(
                    bank[:], Wo[:, k * D + mo * 128: k * D + (mo + 1) * 128],
                    y2[:, k * TT:(k + 1) * TT], start=(k == 0), stop=(k == NCT - 1)),
                    waits=mywait if k == 0 else (), inc="sT")
            st = p.cur("sT")
            p.emit("scalar", lambda e, mo=mo, bank=bank: e.activation(
                attn[:, mo * TT:(mo + 1) * TT], bank[:], Act.Identity,
                bias=bout[:, mo:mo + 1]), waits=[("sT", st)], inc="sS")
        Wa = p.snap()

        for tb in range(4):
            for half in range(2):
                for fb in range(4):
                    mo = half * 4 + fb
                    tw = ((list(Wa) + [("sV", p.cur("sV"))])
                          if fb == 0 else ())
                    p.emit("tensor", lambda e, tb=tb, mo=mo, fb=fb: e.transpose(
                        PS[7][:, fb * 128:(fb + 1) * 128],
                        attn[:, mo * TT + tb * 128: mo * TT + (tb + 1) * 128],
                        identf[:]), waits=tw, inc="sT")
                st = p.cur("sT")
                p.emit("vector", lambda e, tb=tb, half=half: e.tensor_tensor(
                    acc[:, tb * D + half * TT: tb * D + (half + 1) * TT],
                    acc[:, tb * D + half * TT: tb * D + (half + 1) * TT],
                    PS[7][:], Alu.add), waits=[("sT", st)], inc="sV")
        sv = p.cur("sV")
        for b in range(4):
            p.emit("sync", lambda e, j=j, b=b: e.dma_start(
                out=o_out[j * TT + b * 128: j * TT + (b + 1) * 128, :],
                in_=acc[:, b * D:(b + 1) * D]),
                waits=[("sV", sv)], inc="dS", incby=16)

    fin = [("dS", p.cur("dS")), ("dG", p.cur("dG"))]
    p.emit("gpsimd", lambda e: e.memset(tmp[:, 0:1], 0.0), waits=fin)

    with es, semstack, nc.Block() as block:
        p.realize(block, sems)
    return nc



def _shared_consts(inputs):
    f = np.float32
    cc = {}
    cc["W_in"] = np.ascontiguousarray(inputs["L0_attn_W_in"], f)
    cc["W_dt"] = np.ascontiguousarray(inputs["L0_attn_W_dt"], f)
    cc["W_x16"] = np.ascontiguousarray(inputs["L0_attn_W_x"][:, DS:], f)
    cc["W1"] = np.ascontiguousarray(inputs["L0_ffn_W1"], f)
    cc["W2"] = np.ascontiguousarray(inputs["L0_ffn_W2"], f)
    b_in = inputs["L0_attn_b_in"].astype(f)
    cc["b_in_h"] = np.ascontiguousarray(b_in[:DI].reshape(NCT, 128).T)
    cc["b_in_z"] = np.ascontiguousarray(b_in[DI:].reshape(NCT, 128).T)
    cc["b_dt"] = np.ascontiguousarray(inputs["L0_attn_b_dt"].astype(f).reshape(NCT, 128).T)
    bx = np.zeros((128, 1), f)
    bx[:DS, 0] = inputs["L0_attn_b_x"].astype(f)[DS:]
    cc["b_x16"] = bx
    cc["conv_w"] = np.ascontiguousarray(
        inputs["L0_attn_conv_w"].astype(f)[:, 0, :].reshape(NCT, 128, 4)
        .transpose(1, 0, 2).reshape(128, NCT * 4))
    cc["conv_b"] = np.ascontiguousarray(inputs["L0_attn_conv_b"].astype(f).reshape(NCT, 128).T)
    cc["A_log"] = np.ascontiguousarray(
        inputs["L0_attn_A_log"].astype(f).reshape(NCT, 128, DS)
        .transpose(1, 0, 2).reshape(128, NCT * DS))
    cc["Dv"] = np.ascontiguousarray(inputs["L0_attn_D"].astype(f).reshape(NCT, 128).T)
    cc["b1"] = np.ascontiguousarray(inputs["L0_ffn_b1"].astype(f).reshape(KFF, 128).T)
    cc["theta"] = np.ascontiguousarray(
        inputs["L0_ffn_quantum_phases"].astype(f).reshape(KFF, 128).T)
    cc["wreal"] = np.ascontiguousarray(
        inputs["L0_ffn_quantum_weights_real"].astype(f).reshape(KFF, 128).T)
    cc["wimag"] = np.ascontiguousarray(
        inputs["L0_ffn_quantum_weights_imag"].astype(f).reshape(KFF, 128).T)
    cc["b2"] = np.ascontiguousarray(inputs["L0_ffn_b2"].astype(f).reshape(KD, 128).T)
    cc["identbf"] = np.eye(128, dtype=ml_dtypes.bfloat16)
    cc["identf"] = np.eye(128, dtype=f)
    sels = np.zeros((DS, DS * 128), dtype=ml_dtypes.bfloat16)
    for s in range(DS):
        sels[s, s * 128:(s + 1) * 128] = 1
    cc["sels"] = sels
    return cc


def kernel(**inputs):
    f = np.float32
    x = np.ascontiguousarray(inputs["L0_attn_x"], f).reshape(2 * 16384, D)
    cc = _shared_consts(inputs)

    nc1 = build_L1()
    in_maps = []
    for k in range(NCORES):
        sq = k % 4
        m = dict(cc)
        m["x"] = np.ascontiguousarray(x[k * TCORE:(k + 1) * TCORE])
        m["xhalo"] = (np.ascontiguousarray(x[k * TCORE - 128:k * TCORE])
                      if sq > 0 else np.zeros((128, D), f))
        m["halo_mask"] = np.full((128, 1), 1.0 if sq > 0 else 0.0, f)
        in_maps.append(m)
    r1 = run_bass_kernel_spmd(nc1, in_maps, core_ids=list(range(NCORES)))

    nc2 = build_L2()
    in_maps2 = []
    for k in range(NCORES):
        sq = k % 4
        grp = (k // 4) * 4
        exg = np.stack([np.asarray(r1.results[grp + i]["o_exch"], f) for i in range(4)])
        pm = np.zeros((128, 4), f)
        pm[:, :sq] = 1.0
        m = {
            "exg": np.ascontiguousarray(exg),
            "pmask": pm,
            "sp_delta": np.ascontiguousarray(r1.results[k]["o_delta"]),
            "sp_y1": np.ascontiguousarray(r1.results[k]["o_y1"]),
            "sp_siluz": np.ascontiguousarray(r1.results[k]["o_siluz"]),
            "sp_acc": np.ascontiguousarray(r1.results[k]["o_acc"], f),
            "W_out": np.ascontiguousarray(inputs["L0_attn_W_out"], f),
            "b_out": np.ascontiguousarray(
                inputs["L0_attn_b_out"].astype(f).reshape(KD, 128).T),
            "A_log": cc["A_log"],
            "identf": cc["identf"],
        }
        in_maps2.append(m)
    r2 = run_bass_kernel_spmd(nc2, in_maps2, core_ids=list(range(NCORES)))

    out = np.concatenate([np.asarray(r2.results[k]["o_out"], f) for k in range(NCORES)],
                         axis=0)
    return out.reshape(2, 16384, D)
